# version 05 (final)
# speedup vs baseline: 2.0392x; 1.0341x over previous
import math
import numpy as np
import ml_dtypes
import concourse.bass as bass
import concourse.mybir as mybir
from concourse.bass_utils import run_bass_kernel_spmd
from concourse import library_config
from bass_rust import IndirectOffsetOnAxis

F32 = mybir.dt.float32
BF16 = mybir.dt.bfloat16
I32 = mybir.dt.int32
U32 = mybir.dt.uint32
I16 = mybir.dt.int16
U8 = mybir.dt.uint8
AF = mybir.ActivationFunctionType
AL = mybir.AluOpType
AX = mybir.AxisListType
NPBF = ml_dtypes.bfloat16

N_CORES = 8
D = 1024
NTOK = 2048
H = 16
NL = 64
NEXP, CAP = 8, 5120
FFN = 4096
NS_ITERS = 16
REFINE = 3
KB = 1024
TGRP = 512
NGRP = CAP // TGRP

_CPF_TRI = 0
_CPF_EYER = 128
_CPF_RTOK = 192
_CPF_CV = 320
_CPF_BV = 336
_CPF_BO = 1360
_CPF_L1W = 2384
_CPF_L1B = 3408
_CPF_L2W = 4432
_CPF_L2B = 5456
_CPF_BQC = 6480
_CPF_BKC = 6488
_CPF_ONES = 6496
_CPF_IDF = 6752
CPF = 6752 + 128
_CPB_ID = 0
_CPB_E2A = 128
_CPB_BLK2 = 256
_CPB_ONES = 258
CPB = 258 + 256


class Arena:
    def __init__(self, nc, name, nbytes):
        self.h = nc.sbuf_tensor(name, [128, nbytes], U8).__enter__()
        self.nb = nbytes
        self.name = name

    def at(self, off_b, shape, dtype):
        nelem = int(np.prod(shape[1:]))
        nbytes = nelem * mybir.dt.size(dtype)
        assert off_b % 4 == 0 and off_b + nbytes <= self.nb, \
            f"{self.name}: {off_b}+{nbytes} > {self.nb}"
        apv = self.h[:, off_b:off_b + nbytes].bitcast(dtype)
        if len(shape) > 2:
            dims = " ".join(f"d{i}" for i in range(1, len(shape)))
            kw = {f"d{i}": shape[i] for i in range(1, len(shape) - 1)}
            apv = apv.rearrange(f"p ({dims}) -> p {dims}", **kw)
        if shape[0] != 128:
            apv = apv[0:shape[0]]
        return apv


class Chain:
    def __init__(self, nc, block):
        self.nc = nc
        self.block = block
        self.go = nc.semaphore("go").__enter__()
        self.n = 0
        self.dma = {}

    def dsem(self, name):
        if name not in self.dma:
            self.dma[name] = [self.nc.semaphore(name).__enter__(), 0]
        return self.dma[name]

    def dtick(self, instr, name):
        s = self.dsem(name)
        s[1] += 16
        instr.then_inc(s[0], 16)
        return s[1]

    def dval(self, name):
        return self.dsem(name)[1]

    def grp(self, engine, fn, dwaits=()):
        idx = self.n
        self.n += 1
        go = self.go
        waits = [(nm, self.dsem(nm)[1]) for nm, _ in dwaits]

        def body(eng):
            eng.wait_ge(go, idx)
            for nm, val in waits:
                eng.wait_ge(self.dsem(nm)[0], val)
            fn(eng)
            eng.drain()
            eng.nop().then_inc(go, 1)

        getattr(self.block, engine)(body)

    def par(self, specs):
        import os
        if os.environ.get("KSERIAL", ""):
            for engine, fn, dwaits in specs:
                self.grp(engine, fn, dwaits=[(nm, 0) for nm in dwaits])
            return
        base = self.n
        self.n += len(specs)
        go = self.go
        lv = {nm: s[1] for nm, s in self.dma.items()}
        for engine, fn, dwaits in specs:
            waits = [(nm, lv.get(nm, 0)) for nm in dwaits]

            def body(eng, fn=fn, waits=waits):
                eng.wait_ge(go, base)
                for nm, val in waits:
                    eng.wait_ge(self.dsem(nm)[0], val)
                fn(eng)
                eng.drain()
                eng.nop().then_inc(go, 1)

            getattr(self.block, engine)(body)


def build(nc, taps=(), stop_after="all"):
    dram = nc.dram_tensor
    taps = set(taps)

    xr = dram("xr", [NTOK, D], F32, kind="ExternalInput").ap()
    xt = dram("xt", [D, NTOK], F32, kind="ExternalInput").ap()
    wq_t = dram("wq_t", [D, D], F32, kind="ExternalInput").ap()
    wk_t = dram("wk_t", [D, D], F32, kind="ExternalInput").ap()
    wv_t = dram("wv_t", [D, D], F32, kind="ExternalInput").ap()
    wo_t = dram("wo_t", [D, D], F32, kind="ExternalInput").ap()
    qkvw_t = dram("qkvw_t", [D, 2048], F32, kind="ExternalInput").ap()
    proj_t = dram("proj_t", [D, D], F32, kind="ExternalInput").ap()
    rww = dram("rww", [D, NEXP], F32, kind="ExternalInput").ap()
    w1e = dram("w1e", [D, FFN], F32, kind="ExternalInput").ap()
    w2e = dram("w2e", [FFN, D], F32, kind="ExternalInput").ap()
    lmidxT = dram("lmidxT", [NL], I32, kind="ExternalInput").ap()
    tempr = dram("tempr", [1], F32, kind="ExternalInput").ap()
    cpf = dram("cpf", [128, CPF], F32, kind="ExternalInput").ap()
    cpb = dram("cpb", [128, CPB], BF16, kind="ExternalInput").ap()
    out = dram("out", [NTOK, D], F32, kind="ExternalOutput").ap()

    tap_out = {}

    def tap(name, shape, dtype=F32):
        if name in taps:
            tap_out[name] = dram("tap_" + name, list(shape), dtype,
                                 kind="ExternalOutput").ap()
        return tap_out.get(name)

    lm_in = dram("lm_in", [128, D], F32)
    lm_out = dram("lm_out", [128, D], F32)
    zs_in = dram("zs_in", [64, 32 * 65], F32)
    zs_out = dram("zs_out", [64, 32 * 65], F32)
    ag1_in = dram("ag1_in", [NTOK, D], BF16)
    ag1_out = dram("ag1_out", [N_CORES * NTOK + 1, D], BF16)
    ag2_in = dram("ag2_in", [NTOK, NEXP], F32)
    ag2_out = dram("ag2_out", [N_CORES * NTOK, NEXP], F32)
    agE_in = dram("agE_in", [CAP, D], BF16)
    agE_out = dram("agE_out", [NEXP * CAP + 1, D], BF16)
    hsp = dram("hsp", [NTOK, D], F32)
    qkvw_b = dram("qkvw_b", [D, 2048], BF16)
    knscr = dram("knscr", [16 * NTOK], F32)
    scr_pr = dram("scr_pr", [32768], F32)
    scr_pg = dram("scr_pg", [32768], F32)
    scr_ps = dram("scr_ps", [32768], F32)
    scr_own = dram("scr_own", [4096], F32)
    scr_g = dram("scr_g", [CAP], F32)
    scr_i16 = dram("scr_i16", [CAP], I16)
    lnscr = dram("lnscr", [2048], F32)

    arX = Arena(nc, "arX", 64 * KB)
    arA = Arena(nc, "arA", 32 * KB)
    arB = Arena(nc, "arB", 32 * KB)
    arQ = Arena(nc, "arQ", 32 * KB)
    arC = Arena(nc, "arC", 11776)
    arS = Arena(nc, "arS", 29056)
    wbufA = Arena(nc, "wbufA", 2 * KB)
    wbufB = Arena(nc, "wbufB", 2 * KB)
    arM = Arena(nc, "arM", 3904)

    v_wb = [wbufA.at(0, [128, 1024], BF16), wbufB.at(0, [128, 1024], BF16)]

    _co = [0]

    def calloc(shape, dtype):
        nelem = int(np.prod(shape[1:]))
        nbytes = (nelem * mybir.dt.size(dtype) + 255) // 256 * 256
        v = arC.at(_co[0], shape, dtype)
        _co[0] += nbytes
        return v

    c_eyeR = calloc([64, 64], F32)
    c_cv = calloc([128, 16], F32)
    c_bc0 = calloc([128, 1024], F32)
    c_bc1 = calloc([128, 1024], F32)
    c_bqc = calloc([128, 8], F32)
    c_bkc = calloc([128, 8], F32)
    c_onesf = calloc([128, 256], F32)
    c_id = calloc([128, 128], BF16)
    c_2eyeA = calloc([128, 128], BF16)
    c_blk2 = calloc([128, 2], BF16)
    c_onesb = calloc([128, 256], BF16)
    c_invT = calloc([128, 1], F32)
    assert _co[0] <= arC.nb, _co[0]
    c_bvBC, c_boBC = c_bc0, c_bc1

    s_wres = arS.at(0, [128, 8, 1024], BF16)
    c_idf = arQ.at(1024, [128, 128], F32)
    s_ZS = arS.at(16 * KB, [64, 32, 65], F32)
    s_Yb = arS.at(24704, [128, 16, 65], BF16)
    s_lnN2 = arS.at(26880, [128, 32], F32)
    s_lnormN = arS.at(28928, [128, 16], F32)
    s_Wf = arA.at(0, [64, 32, 64], F32)
    s_Xf = arA.at(8 * KB, [64, 32, 64], F32)
    s_Wb = arA.at(16 * KB, [128, 16, 64], BF16)
    s_Xb = arA.at(18 * KB, [128, 16, 64], BF16)
    s_Yf = arA.at(20 * KB, [64, 32, 65], F32)
    s_Rf = arA.at(29 * KB, [64, 32, 65], F32) if False else None

    v_xt = arA.at(0, [128, 8, NTOK], BF16)
    v_kt = arB.at(0, [128, 8, NTOK], BF16)
    v_qt = arQ.at(0, [128, 8, NTOK], BF16)
    v_v = arX.at(0, [128, 16, D], BF16)
    AXH = 32 * KB
    v_lmx = arX.at(AXH, [128, D], F32)
    v_lmxb = arX.at(AXH + 4 * KB, [128, 8, 128], BF16)
    v_kl = arX.at(AXH + 6 * KB, [128, D], F32)
    v_sq = arX.at(AXH + 10 * KB, [128, 2048], BF16)
    v_klT = arX.at(AXH + 14 * KB, [128, 8, 128], BF16)
    v_klT1 = arX.at(AXH + 16 * KB, [128, 8, 128], BF16)
    v_lnorm = arX.at(AXH + 18 * KB, [128, 16], F32)
    v_lnBC = [arX.at(AXH + 19 * KB, [128, 1024], F32),
              arX.at(AXH + 23 * KB, [128, 1024], F32)]
    v_knst = arX.at(AXH, [2, NTOK], F32)
    v_phik = arX.at(AXH, [128, 1024], BF16)
    v_vta = arX.at(AXH + 2 * KB, [128, 1040], BF16)
    v_zacc = arX.at(AXH + 10 * KB, [128, 520], F32)
    v_ubA = arX.at(AXH + 27 * KB, [128, 128], BF16)
    v_ubB = arX.at(AXH + 27 * KB + 256, [128, 128], BF16)
    v_x1 = arX.at(0, [128, 16, D], F32)
    w_lnrow = arX.at(AXH + 27 * KB + 512, [1, 16, 64], F32)
    s_katile = wbufA.at(0, [128, 16, 16], F32)

    s_Rf = arB.at(0, [64, 32, 65], F32)
    v_phiQ = arB.at(0, [128, 8, 1024], BF16)

    v_atto = arB.at(16 * KB, [128, 1024], BF16)
    v_attoT = arB.at(18 * KB, [128, 8, 128], BF16)
    v_xrch = arB.at(20 * KB, [128, 1024], F32)
    v_tmp2 = arB.at(24 * KB, [128, 1024], F32)
    v_tmp3 = arB.at(28 * KB, [128, 1024], F32)

    w_lidxi = arM.at(0, [128, 1], I32)
    w_lidxf = arM.at(256, [128, 1], F32)
    w_tmpf = arM.at(512, [128, 1], F32)
    w_lidxu = arM.at(768, [128, 1], U32)
    w_lmask = arM.at(1024, [128, 1], F32)
    w_rowsum = arM.at(1280, [64, 32], F32)
    w_tot = arM.at(1536, [32, 1], F32)
    w_abcT = arM.at(1792, [1, 32], F32)
    w_abc = arM.at(2048, [64, 32], F32)
    c_eps = arM.at(3872, [128, 1], F32)
    c_idf2 = arM.at(2880, [32, 32], F32)

    ps0 = nc.psum_tensor("ps0", [128, 1024], F32).__enter__()
    ps1 = nc.psum_tensor("ps1", [128, 1024], F32).__enter__()
    psz = nc.psum_tensor("psz", [128, 1024], F32).__enter__()
    pst = nc.psum_tensor("pst", [128, 1024], BF16).__enter__()
    pss = nc.psum_tensor("pss", [128, 128], F32).__enter__()

    block = nc.Block().__enter__()
    ch = Chain(nc, block)
    G = ch.grp
    P = ch.par

    def colh(hh):
        return 512 * (hh % 2) + (hh // 2) * 64

    def mmN(t, psap, lhsT, rhs, start, stop, n=512, tile_position=None):
        N = rhs.shape[-1]
        M = lhsT.shape[-1]
        for i in range(0, N, n):
            w = min(n, N - i)
            kw = {}
            if tile_position is not None:
                kw["tile_position"] = tile_position
            t.matmul(psap[0:M, i:i + w], lhsT, rhs[:, i:i + w], start=start,
                     stop=stop, **kw)

    def emit_taps(specs):
        for nm, ap, shape, dt in specs:
            tv = tap(nm, shape, dt)
            if tv is not None:
                def stt(s, tv=tv, ap=ap):
                    a = ap
                    if len(a.shape) == 3:
                        a = a.rearrange("p a b -> p (a b)")
                    ch.dtick(s.dma_start(out=tv, in_=a), "ds")
                    return None
                G("sync", stt,
                  dwaits=[(nm2, 0) for nm2 in list(ch.dma.keys())])

    def finish():
        def fin(s):
            s.wait_ge(ch.dsem("ds")[0], ch.dval("ds"))
            return None
        G("sync", fin)
        return tap_out

    def ld(s):
        for dst, sl in [(c_cv, (_CPF_CV, 16)), (c_bvBC, (_CPF_BV, 1024)),
                        (c_boBC, (_CPF_BO, 1024)), (c_bqc, (_CPF_BQC, 8)),
                        (c_bkc, (_CPF_BKC, 8)), (c_onesf, (_CPF_ONES, 256))]:
            ch.dtick(s.dma_start(out=dst, in_=cpf[:, sl[0]:sl[0] + sl[1]]), "dl")
        ch.dtick(s.dma_start(out=c_eyeR, in_=cpf[0:64, _CPF_EYER:_CPF_EYER + 64]),
                 "dl")
        for dst, sl in [(c_id, (_CPB_ID, 128)), (c_2eyeA, (_CPB_E2A, 128)),
                        (c_blk2, (_CPB_BLK2, 2)), (c_onesb, (_CPB_ONES, 256))]:
            ch.dtick(s.dma_start(out=dst, in_=cpb[:, sl[0]:sl[0] + sl[1]]), "dl")
        ch.dtick(s.dma_start(out=c_idf2, in_=cpf[0:32, _CPF_IDF:_CPF_IDF + 32]),
                 "dl")
        ch.dtick(s.dma_start(out=c_invT[0:1, :], in_=tempr.unsqueeze(0)), "dl")
        ch.dtick(s.dma_start(out=w_lidxi[0:64, :], in_=lmidxT.unsqueeze(1)), "dl")
        ch.dtick(s.dma_start(out=w_lidxi[64:128, :], in_=lmidxT.unsqueeze(1)), "dl")
        return None
    G("sync", ld)
    DL_CONST = ch.dval("dl")

    def gload(g, dst, src):
        return ch.dtick(g.dma_start(out=dst, in_=src.rearrange(
            "(c p) n -> p c n", p=128)), "dg")

    G("gpsimd", lambda g: g.memset(c_eps, 1e-5))
    G("gpsimd", lambda g: gload(g, v_xt, xt) and None)
    DG_XT = ch.dval("dg")
    G("gpsimd", lambda g: gload(g, s_wres, wk_t) and None)
    DG_WK = ch.dval("dg")

    def pre1(v):
        v.reciprocal(c_invT[0:1, :], c_invT[0:1, :])
        v.tensor_copy(w_lidxf, w_lidxi)
        return None
    G("vector", pre1, dwaits=[("dl", DL_CONST)])
    G("tensor", lambda t: t.matmul(pss[:, 0:1], c_onesf[0:1, 0:128], c_invT[0:1, :],
                                   start=True, stop=True))
    G("vector", lambda v: v.tensor_copy(c_invT, pss[:, 0:1]))

    c1024 = c_cv[:, 2:3]

    def lmi1(v):
        v.tensor_tensor(w_lmask, w_lidxf, c1024, AL.is_ge)
        v.tensor_tensor(w_tmpf, w_lidxf, c1024, AL.subtract)
        return None
    G("vector", lmi1)

    G("vector", lambda v: v.tensor_scalar(w_lidxf, w_tmpf, 1024.0, None,
                                          AL.is_lt))
    G("vector", lambda v: v.tensor_scalar_add(w_tmpf[64:128, :],
                                              w_tmpf[64:128, :], 1024.0))
    G("vector", lambda v: v.tensor_tensor(w_lmask, w_lmask, w_lidxf, AL.mult))
    G("vector", lambda v: v.tensor_tensor(w_tmpf, w_tmpf, w_lmask, AL.mult))
    G("vector", lambda v: v.tensor_copy(w_lidxu, w_tmpf))

    def lmg(g):
        ch.dtick(g.indirect_dma_start(out=v_lmx, out_offset=None, in_=xr,
                                      in_offset=IndirectOffsetOnAxis(w_lidxu, 0)),
                 "dg")
        return None
    G("gpsimd", lmg)
    DG_LMG = ch.dval("dg")
    G("vector", lambda v: v.tensor_scalar(v_lmx, v_lmx, w_lmask, None, AL.mult),
      dwaits=[("dg", DG_LMG)])

    def lmar(g):
        ch.dtick(g.dma_start(out=lm_in[:, :], in_=v_lmx), "dg")
        g.wait_ge(ch.dsem("dg")[0], ch.dval("dg"))
        cs = ch.dsem("cc")
        g.collective_compute(
            "AllReduce", AL.add, replica_groups=[list(range(N_CORES))],
            ins=[lm_in.ap().opt()], outs=[lm_out.ap().opt()]).then_inc(cs[0], 1)
        cs[1] += 1
        g.wait_ge(cs[0], cs[1])
        ch.dtick(g.dma_start(out=v_lmx, in_=lm_out[:, :]), "dg")
        return None
    G("gpsimd", lmar)
    DG_LMAR = ch.dval("dg")

    def proj_fm(dst, bias_col, wdep):
        for nc8 in range(8):
            def mm(t, nc8=nc8):
                for ds in range(8):
                    wl = s_wres[:, ds, nc8 * 128:(nc8 + 1) * 128]
                    mmN(t, ps0, wl, v_xt[:, ds, 0:1024], ds == 0, ds == 7)
                    mmN(t, ps1, wl, v_xt[:, ds, 1024:2048], ds == 0, ds == 7)
                return None
            G("tensor", mm, dwaits=[("dg", wdep), ("dg", DG_XT)])

            def evac(v, nc8=nc8):
                v.tensor_scalar(dst[:, nc8, 0:1024], ps0[:, 0:1024],
                                bias_col[:, nc8:nc8 + 1], None, AL.add)
                v.tensor_scalar(dst[:, nc8, 1024:2048], ps1[:, 0:1024],
                                bias_col[:, nc8:nc8 + 1], None, AL.add)
                return None
            G("vector", evac)

    proj_fm(v_kt, c_bkc, DG_WK)

    G("vector", lambda v: v.tensor_copy(v_sq[:, 0:1024], v_lmx),
      dwaits=[("dg", DG_LMAR)])
    for cc8 in range(8):
        G("tensor", lambda t, cc8=cc8: t.transpose(
            pst[:, 0:128], v_sq[:, cc8 * 128:(cc8 + 1) * 128], c_id))
        G("vector", lambda v, cc8=cc8: v.tensor_copy(v_lmxb[:, cc8, :],
                                                     pst[:, 0:128]))

    def mml(t):
        for ds in range(8):
            mmN(t, ps0, v_lmxb[:, ds, :], s_wres[:, ds, 0:1024], ds == 0, ds == 7)
        return None
    G("tensor", mml)
    G("vector", lambda v: v.tensor_copy(v_kl, ps0[:, 0:1024]))

    G("gpsimd", lambda g: gload(g, s_wres, wq_t) and None)
    DG_WQ = ch.dval("dg")
    proj_fm(v_qt, c_bqc, DG_WQ)

    G("vector", lambda v: v.tensor_copy(v_sq[:, 0:1024], v_kl))
    for cc8 in range(8):
        G("tensor", lambda t, cc8=cc8: t.transpose(
            pst[:, 0:128], v_sq[:, cc8 * 128:(cc8 + 1) * 128], c_id))
        G("vector", lambda v, cc8=cc8: v.tensor_scalar(
            v_klT[:, cc8, :], pst[:, 0:128], c_bkc[:, cc8:cc8 + 1], None, AL.add))
    G("vector", lambda v: v.tensor_scalar(
        v_klT1.rearrange("p a b -> p (a b)"),
        v_klT.rearrange("p a b -> p (a b)"), c_invT, None, AL.mult))
    for cc8 in range(8):
        G("tensor", lambda t, cc8=cc8: t.transpose(pst[:, 0:128],
                                                   v_klT[:, cc8, :], c_id))
        G("vector", lambda v, cc8=cc8: v.tensor_copy(
            v_sq[:, cc8 * 128:(cc8 + 1) * 128], pst[:, 0:128]))
    G("scalar", lambda sc: sc.activation(out=v_kl, in_=v_sq[:, 0:1024],
                                         func=AF.Square))
    G("vector", lambda v: v.tensor_reduce(
        v_lnorm, v_kl.rearrange("p (h d) -> p h d", h=16), AX.X, AL.add))
    G("vector", lambda v: v.tensor_scalar(v_lnorm, v_lnorm, c_invT, None, AL.mult))
    G("vector", lambda v: v.tensor_scalar_mul(s_lnormN, v_lnorm, -1.0))

    def lnst(s):
        ch.dtick(s.dma_start(out=lnscr.rearrange("(p h) -> p h", p=128),
                             in_=v_lnorm), "ds")
        return None
    G("sync", lnst)
    DS_LN = ch.dval("ds")

    for b in range(2):
        def lnld(s, b=b):
            src = lnscr.rearrange("(b2 k j q) -> b2 q j k", b2=2, k=64, j=8)[b]
            flat = w_lnrow.rearrange("p a b -> p (a b)")
            for q in range(2):
                ch.dtick(s.dma_start(out=flat[:, q * 512:(q + 1) * 512],
                                     in_=src[q].unsqueeze(0),
                                     allow_slow_non_contiguous=True), "dl")
            return None
        G("sync", lnld, dwaits=[("ds", DS_LN)])
        dlln = ch.dval("dl")
        G("tensor", lambda t: mmN(t, ps0, c_onesf[0:1, 0:128],
                                  w_lnrow.rearrange("p a b -> p (a b)"),
                                  True, True),
          dwaits=[("dl", dlln)])
        G("vector", lambda v, b=b: v.tensor_copy(v_lnBC[b], ps0[:, 0:1024]))

    G("gpsimd", lambda g: gload(g, s_wres, wv_t) and None)
    DG_WV = ch.dval("dg")
    for tc in range(16):
        def mmv(t, tc=tc):
            for ds in range(8):
                mmN(t, ps0, v_xt[:, ds, tc * 128:(tc + 1) * 128],
                    s_wres[:, ds, 0:1024], ds == 0, ds == 7)
            return None
        G("tensor", mmv, dwaits=[("dg", DG_WV)])
        G("vector", lambda v, tc=tc: v.tensor_tensor(v_v[:, tc, :], ps0[:, 0:1024],
                                                     c_bvBC, AL.add))

    for cc8 in range(8):
        G("scalar", lambda sc, cc8=cc8: sc.activation(
            out=v_sq[:, 0:2048], in_=v_kt[:, cc8, :], func=AF.Square))

        def kmm(t, cc8=cc8):
            mmN(t, psz, c_blk2, v_sq[:, 0:1024], True, True)
            mmN(t, ps1, c_blk2, v_sq[:, 1024:2048], True, True)
            return None
        G("tensor", kmm)

        def kev(v, cc8=cc8):
            v.tensor_copy(v_knst[:, 0:1024], psz[0:2, 0:1024])
            v.tensor_copy(v_knst[:, 1024:2048], ps1[0:2, 0:1024])
            return None
        G("vector", kev, dwaits=[("ds", ch.dval("ds"))])

        def knstr(s, cc8=cc8):
            ch.dtick(s.dma_start(
                out=knscr.rearrange("(p t) -> p t", p=16)[2 * cc8:2 * cc8 + 2, :],
                in_=v_knst), "ds")
            return None
        G("sync", knstr)
    DS_KN = ch.dval("ds")

    def knld(s):
        a3 = knscr.rearrange("(h j q) -> j q h", h=16, j=16)
        for j in range(16):
            ch.dtick(s.dma_start(out=s_katile[:, j, :], in_=a3[j, :, :],
                                 allow_slow_non_contiguous=True), "dl")
        return None
    G("sync", knld, dwaits=[("ds", DS_KN)])
    DL_KA = ch.dval("dl")
    G("vector", lambda v: v.tensor_scalar(
        s_katile.rearrange("p a b -> p (a b)"),
        s_katile.rearrange("p a b -> p (a b)"), c_invT, None, AL.mult),
      dwaits=[("dl", DL_KA)])
    G("scalar", lambda sc: sc.activation(
        out=s_katile.rearrange("p a b -> p (a b)"),
        in_=s_katile.rearrange("p a b -> p (a b)"), func=AF.Exp, scale=-1.0))

    for b in range(2):
        for hh in range(H):
            cc8, par = hh // 2, hh % 2

            def wmm(t, b=b, cc8=cc8, par=par):
                lh1 = v_klT1[64 * par:64 * par + 64, cc8, 64 * b:64 * b + 64]
                lh = v_klT[64 * par:64 * par + 64, cc8, 64 * b:64 * b + 64]
                return t.matmul(pss[0:64, 0:64], lh1, lh, start=True, stop=True,
                                tile_position=(64 * par, 0))
            G("tensor", wmm)

            def wep(v, b=b, hh=hh):
                v.scalar_tensor_tensor(s_Wf[0:64, b * 16 + hh, :], pss[0:64, 0:64],
                                       2.0,
                                       v_lnBC[b][0:64, colh(hh):colh(hh) + 64],
                                       AL.mult, AL.subtract)
                return None
            G("vector", wep)

            def wex(sc, b=b, hh=hh):
                sc.activation(out=s_Wf[0:64, b * 16 + hh, :],
                              in_=s_Wf[0:64, b * 16 + hh, :], func=AF.Exp,
                              bias=s_lnormN[64 * b:64 * b + 64, hh:hh + 1],
                              scale=1.0)
                return None
            G("scalar", wex)
            G("vector", lambda v, b=b, hh=hh: v.tensor_tensor(
                s_Wf[0:64, b * 16 + hh, :], s_Wf[0:64, b * 16 + hh, :], c_eyeR,
                AL.add))

    if stop_after == "wker":
        emit_taps([("wker", s_Wf, [64, 32 * 64], F32)])
        return finish()

    if stop_after == "qkv":
        emit_taps([
            ("kt", v_kt, [128, 8 * NTOK], BF16),
            ("qt", v_qt, [128, 8 * NTOK], BF16),
            ("v", v_v, [128, 16 * D], BF16),
            ("klT", v_klT, [128, 8 * 128], BF16),
            ("lnorm", v_lnorm, [128, 16], F32),
            ("lnbc0", v_lnBC[0], [128, 1024], F32),
            ("katile", s_katile, [128, 256], F32),
            ("wker", s_Wf, [64, 32 * 64], F32)])
        return finish()

    import os
    NTC = 1 if stop_after == "zloop1" else 16
    ZP = int(os.environ.get("KZP", "9"))
    for tc in range(NTC):
        b = tc // 8

        def gkm(t, tc=tc, b=b):
            for hh in range(H):
                cc8, par = hh // 2, hh % 2
                kl = v_kt[64 * par:64 * par + 64, cc8, tc * 128:(tc + 1) * 128]
                lh1 = v_klT1[64 * par:64 * par + 64, cc8, 64 * b:64 * b + 64]
                t.matmul(ps0[:, colh(hh):colh(hh) + 64], kl, lh1, start=True,
                         stop=True, tile_position=(64 * par, 0))
            return None
        if ZP >= 1:
            G("tensor", gkm)

        def pke(v, b=b):
            v.scalar_tensor_tensor(v_phik, ps0[:, 0:1024], 2.0, v_lnBC[b],
                                   AL.mult, AL.subtract)
            return None
        if ZP >= 2:
            G("vector", pke)
        if ZP >= 3:
            G("scalar", lambda sc: sc.activation(out=v_phik, in_=v_phik,
                                                 func=AF.Exp))

        def vta(v, tc=tc):
            ka = s_katile[:, tc, :]
            vta3 = v_vta[:, 0:1040].rearrange("p (a b) -> p a b", a=16)
            v.tensor_tensor(vta3[:, :, 0:64],
                            v_v[:, tc, :].rearrange("p (a b) -> p a b", a=16),
                            ka.unsqueeze(2).broadcast_to([128, 16, 64]), AL.mult)
            v.tensor_copy(vta3[:, :, 64:65], ka.unsqueeze(2))
            return None
        if ZP >= 4:
            G("vector", vta)

        def zslot(hh):
            ps = psz if hh < 8 else ps1
            pr = 64 * (hh % 2)
            fo = 512 * (hh % 2) + ((hh // 2) % 4) * 65
            return ps[pr:pr + 64, fo:fo + 65], pr

        def zmm(t, tc=tc):
            for hh in range(H):
                sl, pr = zslot(hh)
                t.matmul(sl, v_phik[:, colh(hh):colh(hh) + 64],
                         v_vta[:, hh * 65:hh * 65 + 65],
                         start=True, stop=True, tile_position=(0, pr))
            return None
        if ZP >= 5:
            G("tensor", zmm)

        if tc % 8 == 0 and ZP >= 6:
            G("gpsimd", lambda g: g.memset(v_zacc, 0.0))

        def zadd(v):
            v.tensor_tensor(v_zacc[0:64, 0:260], v_zacc[0:64, 0:260],
                            psz[0:64, 0:260], AL.add)
            v.tensor_tensor(v_zacc[64:128, 0:260], v_zacc[64:128, 0:260],
                            psz[64:128, 512:772], AL.add)
            v.tensor_tensor(v_zacc[0:64, 260:520], v_zacc[0:64, 260:520],
                            ps1[0:64, 0:260], AL.add)
            v.tensor_tensor(v_zacc[64:128, 260:520], v_zacc[64:128, 260:520],
                            ps1[64:128, 512:772], AL.add)
            return None
        if ZP >= 7:
            G("vector", zadd)

        if tc % 8 == 7:
            def zfin(v, b=b):
                for hh in range(H):
                    pr = 64 * (hh % 2)
                    fo = (hh // 8) * 260 + ((hh // 2) % 4) * 65
                    v.tensor_copy(s_ZS[0:64, b * 16 + hh, :],
                                  v_zacc[pr:pr + 64, fo:fo + 65])
                return None
            G("vector", zfin)

    if stop_after in ("zloop", "zloop1"):
        emit_taps([("zs", s_ZS, [64, 32 * 65], F32)])
        return finish()

    def zar(g):
        ch.dtick(g.dma_start(out=zs_in[:, :],
                             in_=s_ZS.rearrange("p a b -> p (a b)")), "dg")
        g.wait_ge(ch.dsem("dg")[0], ch.dval("dg"))
        cs = ch.dsem("cc")
        g.collective_compute(
            "AllReduce", AL.add, replica_groups=[list(range(N_CORES))],
            ins=[zs_in.ap().opt()], outs=[zs_out.ap().opt()]).then_inc(cs[0], 1)
        cs[1] += 1
        g.wait_ge(cs[0], cs[1])
        ch.dtick(g.dma_start(out=s_ZS.rearrange("p a b -> p (a b)"),
                             in_=zs_out[:, :]), "dg")
        return None
    G("gpsimd", zar)
    DG_ZAR = ch.dval("dg")

    if stop_after == "pass1":
        emit_taps([("zs", s_ZS, [64, 32 * 65], F32),
                   ("wker", s_Wf, [64, 32 * 64], F32)])
        return finish()

    G("vector", lambda v: v.tensor_reduce(w_rowsum, s_Wf, AX.X, AL.add,
                                          apply_absolute_value=True),
      dwaits=[("dg", DG_ZAR)])
    G("tensor", lambda t: t.matmul(pss[0:32, 0:1], w_rowsum, c_onesf[0:64, 0:1],
                                   start=True, stop=True))
    G("vector", lambda v: v.reciprocal(w_tot[:, 0:1], pss[0:32, 0:1]))

    G("tensor", lambda t: t.transpose(ps0[0:1, 0:32], w_tot[:, 0:1],
                                      c_idf2[0:32, 0:32]))
    G("vector", lambda v: v.tensor_copy(w_abcT, ps0[0:1, 0:32]))
    G("tensor", lambda t: t.matmul(pss[0:64, 0:32], c_onesf[0:1, 0:64], w_abcT,
                                   start=True, stop=True))
    G("vector", lambda v: v.tensor_copy(w_abc, pss[0:64, 0:32]))

    def nsx0(v):
        for bh in range(32):
            v.tensor_scalar(s_Xf[0:64, bh, :], s_Wf[0:64, bh, :],
                            w_abc[:, bh:bh + 1], None, AL.mult)
        return None
    G("vector", nsx0)

    def stag(v):
        for i in range(16):
            v.tensor_copy(s_Wb[0:64, i, :], s_Wf[0:64, 2 * i, :])
            v.tensor_copy(s_Wb[64:128, i, :], s_Wf[0:64, 2 * i + 1, :])
            v.tensor_copy(s_Xb[0:64, i, :], s_Xf[0:64, 2 * i, :])
            v.tensor_copy(s_Xb[64:128, i, :], s_Xf[0:64, 2 * i + 1, :])
        return None
    G("vector", stag)

    for it in range(NS_ITERS):
        for pg in range(8):
            iA, iB = 2 * pg, 2 * pg + 1

            def t1(t, iA=iA, iB=iB):
                t.matmul(pss[0:64, 0:64], s_Wb[0:64, iA, :], s_Xb[0:64, iA, :],
                         start=True, stop=True, tile_position=(0, 0))
                t.matmul(pss[64:128, 0:64], s_Wb[64:128, iA, :],
                         s_Xb[64:128, iA, :], start=True, stop=True,
                         tile_position=(64, 64))
                t.matmul(pss[0:64, 64:128], s_Wb[0:64, iB, :], s_Xb[0:64, iB, :],
                         start=True, stop=True, tile_position=(0, 0))
                t.matmul(pss[64:128, 64:128], s_Wb[64:128, iB, :],
                         s_Xb[64:128, iB, :], start=True, stop=True,
                         tile_position=(64, 64))
                return None
            G("tensor", t1)

            def u12(v):
                e = c_2eyeA[0:64, 0:64]
                e2 = c_2eyeA[64:128, 64:128]
                v.scalar_tensor_tensor(v_ubA[0:64, 0:64], pss[0:64, 0:64], -1.0,
                                       e, AL.mult, AL.add)
                v.scalar_tensor_tensor(v_ubA[64:128, 0:64], pss[64:128, 0:64],
                                       -1.0, e2, AL.mult, AL.add)
                v.scalar_tensor_tensor(v_ubB[0:64, 0:64], pss[0:64, 64:128],
                                       -1.0, e, AL.mult, AL.add)
                v.scalar_tensor_tensor(v_ubB[64:128, 0:64], pss[64:128, 64:128],
                                       -1.0, e2, AL.mult, AL.add)
                return None
            G("vector", u12)

            def t2(t, iA=iA, iB=iB):
                t.matmul(pss[0:64, 0:64], s_Xb[0:64, iA, :], v_ubA[0:64, 0:64],
                         start=True, stop=True, tile_position=(0, 0))
                t.matmul(pss[64:128, 0:64], s_Xb[64:128, iA, :],
                         v_ubA[64:128, 0:64], start=True, stop=True,
                         tile_position=(64, 64))
                t.matmul(pss[0:64, 64:128], s_Xb[0:64, iB, :],
                         v_ubB[0:64, 0:64], start=True, stop=True,
                         tile_position=(0, 0))
                t.matmul(pss[64:128, 64:128], s_Xb[64:128, iB, :],
                         v_ubB[64:128, 0:64], start=True, stop=True,
                         tile_position=(64, 64))
                return None
            G("tensor", t2)

            def xcp(v, iA=iA, iB=iB):
                v.tensor_copy(s_Xb[0:64, iA, :], pss[0:64, 0:64])
                v.tensor_copy(s_Xb[64:128, iA, :], pss[64:128, 0:64])
                v.tensor_copy(s_Xb[0:64, iB, :], pss[0:64, 64:128])
                v.tensor_copy(s_Xb[64:128, iB, :], pss[64:128, 64:128])
                return None
            G("vector", xcp)

    def xunstag(v):
        for i in range(16):
            v.tensor_copy(s_Xf[0:64, 2 * i, :], s_Xb[0:64, i, :])
            v.tensor_copy(s_Xf[0:64, 2 * i + 1, :], s_Xb[64:128, i, :])
        return None
    G("vector", xunstag)

    s_Yf2 = s_Yf

    def _rj(j):
        return (j % 4) * 65 + 512 * (j // 4)

    def halfmm(lhs_sel, rhs_sel, outfn):
        for half in range(4):
            def mm(t, half=half):
                for j in range(8):
                    bh = half * 8 + j
                    t.matmul(ps0[0:64, _rj(j):_rj(j) + 65], lhs_sel(bh),
                             rhs_sel(bh), start=True, stop=True)
                return None
            G("tensor", mm)

            def ev(v, half=half):
                for j in range(8):
                    outfn(v, half * 8 + j, ps0[0:64, _rj(j):_rj(j) + 65])
                return None
            G("vector", ev)

    halfmm(lambda bh: s_Xf[0:64, bh, :], lambda bh: s_ZS[0:64, bh, :],
           lambda v, bh, src: v.tensor_copy(s_Yf2[0:64, bh, :], src))
    for _ in range(REFINE):
        halfmm(lambda bh: s_Wf[0:64, bh, :], lambda bh: s_Yf2[0:64, bh, :],
               lambda v, bh, src: v.tensor_tensor(s_Rf[0:64, bh, :],
                                                  s_ZS[0:64, bh, :], src,
                                                  AL.subtract))
        halfmm(lambda bh: s_Xf[0:64, bh, :], lambda bh: s_Rf[0:64, bh, :],
               lambda v, bh, src: v.tensor_tensor(s_Yf2[0:64, bh, :],
                                                  s_Yf2[0:64, bh, :], src, AL.add))

    def ybfill(v):
        for b in range(2):
            for hh in range(H):
                base = 64 * (hh // 8)
                v.tensor_copy(s_Yb[base:base + 64, b * 8 + (hh % 8), :],
                              s_Yf2[0:64, b * 16 + hh, :])
        return None
    G("vector", ybfill)
    def ln2ld(s):
        a2 = lnscr.rearrange("(k h) -> k h", h=16)
        for half in range(2):
            for b in range(2):
                ch.dtick(s.dma_start(
                    out=s_lnN2[64 * half:64 * half + 64, 16 * b:16 * b + 16],
                    in_=a2[64 * b:64 * b + 64, :],
                    allow_slow_non_contiguous=True), "dl")
        return None
    G("sync", ln2ld)
    DL_LN2 = ch.dval("dl")
    G("vector", lambda v: v.tensor_scalar_mul(s_lnN2, s_lnN2, -1.0),
      dwaits=[("dl", DL_LN2)])

    if stop_after == "ns":
        emit_taps([("yf", s_Yf2, [64, 32 * 65], F32)])
        return finish()

    G("gpsimd", lambda g: gload(g, s_wres, wo_t) and None)
    DG_WO = ch.dval("dg")

    for b in range(2):
        for hh in range(H):
            cc8, par = hh // 2, hh % 2

            base = 64 * (hh // 8)

            def pqm1(t, b=b, hh=hh, cc8=cc8, par=par, base=base):
                lh1 = v_klT1[64 * par:64 * par + 64, cc8, 64 * b:64 * b + 64]
                qh = v_qt[64 * par:64 * par + 64, cc8, b * 1024:(b + 1) * 1024]
                ps = ps0 if hh % 2 == 0 else ps1
                for i in range(0, 1024, 512):
                    t.matmul(ps[base:base + 64, i:i + 512], lh1, qh[:, i:i + 512],
                             start=True, stop=True,
                             tile_position=(64 * par, base))
                return None
            G("tensor", pqm1)

            def pqe(sc, b=b, hh=hh, base=base):
                ps = ps0 if hh % 2 == 0 else ps1
                sc.activation(out=v_phiQ[base:base + 64, hh % 8, :],
                              in_=ps[base:base + 64, 0:1024], func=AF.Exp,
                              bias=s_lnN2[base:base + 64, b * 16 + hh:
                                          b * 16 + hh + 1],
                              scale=2.0)
                return None
            G("scalar", pqe)

        JLIST = range(8) if not os.environ.get("KJ1") else [int(os.environ["KJ1"])]
        KP3 = int(os.environ.get("KP3", "9"))
        for j in JLIST:
            full_tail = (KP3 >= 9) or (j != 0)
            tc = b * 8 + j

            def omm(t, b=b, j=j):
                for hh in range(H):
                    ps = ps0 if hh < 8 else ps1
                    fo = _rj(hh % 8)
                    base = 64 * (hh // 8)
                    t.matmul(ps[:, fo:fo + 65],
                             v_phiQ[base:base + 64, hh % 8,
                                    j * 128:(j + 1) * 128],
                             s_Yb[base:base + 64, b * 8 + (hh % 8), :],
                             start=True, stop=True,
                             tile_position=(base, 0))
                return None
            G("tensor", omm)

            def onorm(v):
                for half, ps in ((0, ps0), (1, ps1)):
                    src = ps[:, 0:972].rearrange("p (b2 g) -> p b2 g", b2=2) \
                        if False else None
                    s4 = ps[:, 0:1024].rearrange("p (b2 g) -> p b2 g", b2=2)
                    s4 = s4.rearrange("p b2 (a c) -> p b2 a c", a=7)
                    return None
            def onorm2(v):
                for half, ps in ((0, ps0), (1, ps1)):
                    for b2 in range(2):
                        src = ps[:, 512 * b2:512 * b2 + 260] \
                            .rearrange("p (a c) -> p a c", a=4)[:, :, 64:65]
                        v.tensor_scalar_max(
                            v_tmp2[:, half * 8 + b2 * 4:half * 8 + b2 * 4 + 4]
                            .unsqueeze(2), src, 1e-10)
                return None
            G("vector", onorm2)
            if b == 0 and j == 1:
                tj3 = tap("normj1", [128, 16], F32)
                if tj3 is not None:
                    G("sync", lambda s, tj3=tj3: ch.dtick(
                        s.dma_start(out=tj3, in_=v_tmp2[:, 0:16]), "ds") and None)
            G("vector", lambda v: v.reciprocal(v_tmp2[:, 0:16], v_tmp2[:, 0:16]))

            def odiv(v):
                for half, ps in ((0, ps0), (1, ps1)):
                    for b2 in range(2):
                        src = ps[:, 512 * b2:512 * b2 + 260] \
                            .rearrange("p (a c) -> p a c", a=4)[:, :, 0:64]
                        rec = v_tmp2[:, half * 8 + b2 * 4:half * 8 + b2 * 4 + 4] \
                            .unsqueeze(2).broadcast_to([128, 4, 64])
                        dst = v_atto[:, half * 512 + b2 * 256:
                                     half * 512 + b2 * 256 + 256] \
                            .rearrange("p (a c) -> p a c", a=4)
                        v.tensor_tensor(dst, src, rec, AL.mult)
                return None
            G("vector", odiv)

            tail_tr = full_tail or KP3 >= 2
            tail_mmo = full_tail or KP3 >= 3
            tail_res = full_tail or KP3 >= 4
            if b == 0 and j == 0:
                tj0 = tap("phiQj0", [64, 16 * 1024], BF16)
                if tj0 is not None:
                    G("sync", lambda s, tj0=tj0: ch.dtick(
                        s.dma_start(out=tj0,
                                    in_=v_phiQ.rearrange("p a b -> p (a b)")),
                        "ds") and None)
            if b == 0 and j == 1:
                tj1 = tap("phiQj1", [64, 16 * 1024], BF16)
                if tj1 is not None:
                    G("sync", lambda s, tj1=tj1: ch.dtick(
                        s.dma_start(out=tj1,
                                    in_=v_phiQ.rearrange("p a b -> p (a b)")),
                        "ds") and None)
            if b == 0 and j == 1:
                tj = tap("attoj1", [128, 1024], BF16)
                if tj is not None:
                    G("sync", lambda s: ch.dtick(
                        s.dma_start(out=tj, in_=v_atto[:, 0:1024]), "ds") and None)

            if tail_tr:
                def ptp8(t):
                    for cc8 in range(8):
                        t.transpose(pst[:, cc8 * 128:(cc8 + 1) * 128],
                                    v_atto[:, cc8 * 128:(cc8 + 1) * 128], c_id)
                    return None
                G("tensor", ptp8)
                G("vector", lambda v: v.tensor_copy(
                    v_attoT, pst[:, 0:1024].rearrange("p (a k) -> p a k", a=8)))

            def mmo(t):
                for ds in range(8):
                    mmN(t, psz, v_attoT[:, ds, :], s_wres[:, ds, 0:1024],
                        ds == 0, ds == 7)
                return None
            if tail_mmo:
                G("tensor", mmo, dwaits=[("dg", DG_WO)])
            if not tail_res:
                continue

            KP4 = os.environ.get("KP4", "")
            def xrld(s, tc=tc):
                ch.dtick(s.dma_start(out=v_xrch,
                                     in_=xr[tc * 128:(tc + 1) * 128, :]), "dl")
                return None
            if KP4 != "b":
                G("sync", xrld)
            dlxr = ch.dval("dl")

            if KP4 != "a":
                G("vector", lambda v: v.tensor_tensor(v_tmp3, psz[:, 0:1024],
                                                      c_boBC, AL.add),
                  dwaits=[("dl", dlxr)])
            if b == 0 and j == 1:
                tj2 = tap("wooutj1", [128, 1024], F32)
                if tj2 is not None:
                    G("sync", lambda s, tj2=tj2: ch.dtick(
                        s.dma_start(out=tj2, in_=v_tmp3), "ds") and None)
            if KP4 == "":
                G("vector", lambda v: v.tensor_tensor(v_tmp3, v_tmp3, v_xrch,
                                                      AL.add))
            if not (full_tail or KP3 >= 5):
                continue

            def ln_plain(dst_ap):
                G("vector", lambda v: v.tensor_reduce(v_tmp2[:, 0:1], v_tmp3, AX.X,
                                                      AL.add))
                G("scalar", lambda sc: sc.activation(out=v_xrch, in_=v_tmp3,
                                                     func=AF.Square))
                G("vector", lambda v: v.tensor_reduce(v_tmp2[:, 1:2], v_xrch, AX.X,
                                                      AL.add))
                G("vector", lambda v: v.tensor_scalar_mul(v_tmp2[:, 0:2],
                                                          v_tmp2[:, 0:2],
                                                          1.0 / 1024.0))
                G("vector", lambda v: v.tensor_tensor(v_tmp2[:, 2:3],
                                                      v_tmp2[:, 0:1],
                                                      v_tmp2[:, 0:1], AL.mult))
                G("vector", lambda v: v.tensor_tensor(v_tmp2[:, 1:2],
                                                      v_tmp2[:, 1:2],
                                                      v_tmp2[:, 2:3], AL.subtract))
                G("scalar", lambda sc: sc.activation(out=v_tmp2[:, 2:3],
                                                     in_=v_tmp2[:, 1:2],
                                                     func=AF.Sqrt, bias=c_eps,
                                                     scale=1.0))
                G("vector", lambda v: v.reciprocal(v_tmp2[:, 3:4], v_tmp2[:, 2:3]))
                G("vector", lambda v: v.tensor_scalar(dst_ap, v_tmp3,
                                                      v_tmp2[:, 0:1],
                                                      v_tmp2[:, 3:4], AL.subtract,
                                                      AL.mult))
            ln_plain(v_x1[:, tc, :])

    if stop_after == "pass2":
        emit_taps([("x1", v_x1, [128, 16 * 1024], F32)])
        return finish()

    q_res = ([arQ.at(2 * KB + sl * 2 * KB, [128, 1024], BF16)
              for sl in range(7)]
             + [arB.at(8 * KB + (sl - 7) * 2 * KB, [128, 1024], BF16)
                for sl in range(7, 11)]
             + [arS.at(16384 + (sl - 11) * 2 * KB, [128, 1024], BF16)
                for sl in range(11, 15)]
             + [arS.at(24704, [128, 1024], BF16)])

    def qld(g):
        q4 = qkvw_t.rearrange("(c p) (h n) -> p c h n", p=128, h=2)
        for sl in range(16):
            ch.dtick(g.dma_start(out=q_res[sl], in_=q4[:, sl // 2, sl % 2, :]),
                     "dg")
        return None
    G("gpsimd", qld)
    DG_QR = ch.dval("dg")

    G("gpsimd", lambda g: gload(g, s_wres, proj_t) and None)
    DG_PJ = ch.dval("dg")

    v_rwb = arM.at(6400 - 256 if False else 3200, [128, 8, 8], BF16)
    v_rwf = arM.at(3456, [128, 8, 8], F32)

    def rwld(s):
        ch.dtick(s.dma_start(out=v_rwf,
                             in_=rww.rearrange("(c p) n -> p c n", p=128)), "dl")
        ch.dtick(s.dma_start(out=c_idf, in_=cpf[:, _CPF_IDF:_CPF_IDF + 128]), "dl")
        return None
    G("sync", rwld)
    DL_RW = ch.dval("dl")
    G("vector", lambda v: v.tensor_copy(v_rwb.rearrange("p a b -> p (a b)"),
                                        v_rwf.rearrange("p a b -> p (a b)")),
      dwaits=[("dl", DL_RW)])

    b_ln1a = arA.at(0, [128, 1024], F32)
    b_ln1b = arA.at(4 * KB, [128, 1024], F32)
    b_ln1T = arA.at(8 * KB, [128, 8, 128], BF16)
    b_qkv0 = arA.at(10 * KB, [128, 2048], BF16)
    b_qkv1 = arA.at(14 * KB, [128, 2048], BF16)
    b_att = arA.at(18 * KB, [128, 1024], BF16)
    b_attT = arA.at(20 * KB, [128, 8, 128], BF16)
    b_dots = arA.at(22 * KB, [128, 64], F32)
    b_hi = arA.at(23 * KB, [128, 1024], BF16)
    b_sp = arA.at(27 * KB, [128, 1024], F32)
    b_hiT = arA.at(25 * KB, [128, 8, 128], BF16)
    b_logits = arB.at(0, [16, 2048], F32)

    def ln_affine(src_chunk_ap, dst_f32, dst_b16):
        G("vector", lambda v: v.tensor_reduce(v_tmp2[:, 0:1], src_chunk_ap, AX.X,
                                              AL.add))
        G("scalar", lambda sc: sc.activation(out=v_xrch, in_=src_chunk_ap,
                                             func=AF.Square))
        G("vector", lambda v: v.tensor_reduce(v_tmp2[:, 1:2], v_xrch, AX.X, AL.add))
        G("vector", lambda v: v.tensor_scalar_mul(v_tmp2[:, 0:2], v_tmp2[:, 0:2],
                                                  1.0 / 1024.0))
        G("vector", lambda v: v.tensor_tensor(v_tmp2[:, 2:3], v_tmp2[:, 0:1],
                                              v_tmp2[:, 0:1], AL.mult))
        G("vector", lambda v: v.tensor_tensor(v_tmp2[:, 1:2], v_tmp2[:, 1:2],
                                              v_tmp2[:, 2:3], AL.subtract))
        G("scalar", lambda sc: sc.activation(out=v_tmp2[:, 2:3], in_=v_tmp2[:, 1:2],
                                             func=AF.Sqrt, bias=c_eps, scale=1.0))
        G("vector", lambda v: v.reciprocal(v_tmp2[:, 3:4], v_tmp2[:, 2:3]))
        G("vector", lambda v: v.tensor_scalar(dst_f32, src_chunk_ap,
                                              v_tmp2[:, 0:1], v_tmp2[:, 3:4],
                                              AL.subtract, AL.mult))

    KQOF = 2 * 64
    GR = 256

    for j in range(8):
        def l1ld(s):
            ch.dtick(s.dma_start(out=c_bc0, in_=cpf[:, _CPF_L1W:_CPF_L1W + 1024]),
                     "dl")
            ch.dtick(s.dma_start(out=c_bc1, in_=cpf[:, _CPF_L1B:_CPF_L1B + 1024]),
                     "dl")
            return None
        G("sync", l1ld)
        dll1 = ch.dval("dl")
        for which, tc in ((0, j), (1, j + 8)):
            dst = b_ln1a if which == 0 else b_ln1b
            ln_affine(v_x1[:, tc, :], dst, None)
            G("vector", lambda v, dst=dst: v.tensor_tensor(dst, dst, c_bc0,
                                                           AL.mult),
              dwaits=[("dl", dll1)])
            G("vector", lambda v, dst=dst: v.tensor_tensor(dst, dst, c_bc1,
                                                           AL.add))

        b_ln1Tb = arB.at(18 * KB, [128, 8, 128], BF16)
        for which, tc in ((0, j), (1, j + 8)):
            srcf = b_ln1a if which == 0 else b_ln1b
            dstT = b_ln1T if which == 0 else b_ln1Tb
            G("vector", lambda v, srcf=srcf: v.tensor_copy(v_atto[:, 0:1024], srcf))

            def tp8(t):
                for cc8 in range(8):
                    t.transpose(pst[:, cc8 * 128:(cc8 + 1) * 128],
                                v_atto[:, cc8 * 128:(cc8 + 1) * 128], c_id)
                return None
            G("tensor", tp8)
            G("vector", lambda v, dstT=dstT: v.tensor_copy(
                dstT, pst[:, 0:1024].rearrange("p (a k) -> p a k", a=8)))
        for nh in range(2):
            def qmm(t, nh=nh, j=j):
                for ds in range(8):
                    mmN(t, ps0, b_ln1T[:, ds, :], q_res[ds * 2 + nh],
                        ds == 0, ds == 7)
                    mmN(t, ps1, b_ln1Tb[:, ds, :], q_res[ds * 2 + nh],
                        ds == 0, ds == 7)
                return None
            G("tensor", qmm, dwaits=[("dg", DG_QR)] if j == 0 else [])

            def qev(v, nh=nh):
                v.tensor_copy(b_qkv0[:, nh * 1024:(nh + 1) * 1024], ps0[:, 0:1024])
                v.tensor_copy(b_qkv1[:, nh * 1024:(nh + 1) * 1024], ps1[:, 0:1024])
                return None
            G("vector", qev)

        def dots(v):
            q1 = b_qkv1.rearrange("p (g r) -> p g r", g=8)[:, :, 0:128] \
                .rearrange("p g (s d) -> p g s d", s=2)
            k0 = b_qkv0.rearrange("p (g r) -> p g r", g=8)[:, :, 128:192] \
                .unsqueeze(2).broadcast_to([128, 8, 2, 64])
            v.tensor_tensor(v_tmp3.rearrange("p (g s d) -> p g s d", g=8, s=2),
                            q1, k0, AL.mult)
            return None
        G("vector", dots)
        G("vector", lambda v: v.tensor_reduce(
            b_dots[:, 0:16], v_tmp3.rearrange("p (h d) -> p h d", h=16), AX.X,
            AL.add))

        def dots2(v):
            q1 = b_qkv1.rearrange("p (g r) -> p g r", g=8)[:, :, 0:128] \
                .rearrange("p g (s d) -> p g s d", s=2)
            k1 = b_qkv1.rearrange("p (g r) -> p g r", g=8)[:, :, 128:192] \
                .unsqueeze(2).broadcast_to([128, 8, 2, 64])
            v.tensor_tensor(v_tmp3.rearrange("p (g s d) -> p g s d", g=8, s=2),
                            q1, k1, AL.mult)
            return None
        G("vector", dots2)
        G("vector", lambda v: v.tensor_reduce(
            b_dots[:, 16:32], v_tmp3.rearrange("p (h d) -> p h d", h=16), AX.X,
            AL.add))
        G("vector", lambda v: v.tensor_tensor(b_dots[:, 32:48], b_dots[:, 0:16],
                                              b_dots[:, 16:32], AL.subtract))
        G("scalar", lambda sc: sc.activation(out=b_dots[:, 32:48],
                                             in_=b_dots[:, 32:48], func=AF.Sigmoid,
                                             scale=0.125))

        def vexp(src):
            return src.rearrange("p (g r) -> p g r", g=8)[:, :, 192:256] \
                .unsqueeze(2).broadcast_to([128, 8, 2, 64])

        def hd4(ap):
            return ap.rearrange("p (g s d) -> p g s d", g=8, s=2)

        for which, tc in ((0, j), (1, j + 8)):
            if which == 0:
                G("vector", lambda v: v.tensor_copy(hd4(b_att[:, 0:1024]),
                                                    vexp(b_qkv0)))
            else:
                def att1(v):
                    v.tensor_tensor(hd4(v_tmp3), vexp(b_qkv0), vexp(b_qkv1),
                                    AL.subtract)
                    return None
                G("vector", att1)

                def att2(v):
                    al = b_dots[:, 32:48].unsqueeze(2).broadcast_to([128, 16, 64])
                    v.tensor_tensor(v_tmp3.rearrange("p (h d) -> p h d", h=16),
                                    v_tmp3.rearrange("p (h d) -> p h d", h=16),
                                    al, AL.mult)
                    return None
                G("vector", att2)
                G("vector", lambda v: v.tensor_tensor(hd4(v_tmp3), hd4(v_tmp3),
                                                      vexp(b_qkv1), AL.add))
                G("vector", lambda v: v.tensor_copy(b_att[:, 0:1024], v_tmp3))

            def atp8(t):
                for cc8 in range(8):
                    t.transpose(pst[:, cc8 * 128:(cc8 + 1) * 128],
                                b_att[:, cc8 * 128:(cc8 + 1) * 128], c_id)
                return None
            G("tensor", atp8)
            G("vector", lambda v: v.tensor_copy(
                b_attT, pst[:, 0:1024].rearrange("p (a k) -> p a k", a=8)))

            def pmm(t):
                for ds in range(8):
                    mmN(t, psz, b_attT[:, ds, :], s_wres[:, ds, 0:1024],
                        ds == 0, ds == 7)
                return None
            G("tensor", pmm, dwaits=[("dg", DG_PJ)])

            G("vector", lambda v, tc=tc: v.tensor_tensor(v_tmp3, psz[:, 0:1024],
                                                         v_x1[:, tc, :], AL.add))
            G("vector", lambda v, tc=tc: v.tensor_tensor(b_sp, v_tmp3,
                                                         v_x1[:, tc, :], AL.add),
              dwaits=[("ds", ch.dval("ds"))])

            def hspst(s, tc=tc):
                ch.dtick(s.dma_start(out=hsp[tc * 128:(tc + 1) * 128, :],
                                     in_=b_sp), "ds")
                return None
            G("sync", hspst)
            DS_HSP = ch.dval("ds")

            G("vector", lambda v: v.tensor_reduce(v_tmp2[:, 0:1], v_tmp3, AX.X,
                                                  AL.add))
            G("scalar", lambda sc: sc.activation(out=v_xrch, in_=v_tmp3,
                                                 func=AF.Square))
            G("vector", lambda v: v.tensor_reduce(v_tmp2[:, 1:2], v_xrch, AX.X,
                                                  AL.add))
            G("vector", lambda v: v.tensor_scalar_mul(v_tmp2[:, 0:2],
                                                      v_tmp2[:, 0:2], 1.0 / 1024.0))
            G("vector", lambda v: v.tensor_tensor(v_tmp2[:, 2:3], v_tmp2[:, 0:1],
                                                  v_tmp2[:, 0:1], AL.mult))
            G("vector", lambda v: v.tensor_tensor(v_tmp2[:, 1:2], v_tmp2[:, 1:2],
                                                  v_tmp2[:, 2:3], AL.subtract))
            G("scalar", lambda sc: sc.activation(out=v_tmp2[:, 2:3],
                                                 in_=v_tmp2[:, 1:2], func=AF.Sqrt,
                                                 bias=c_eps, scale=1.0))
            def l2ld(s):
                ch.dtick(s.dma_start(out=c_bc0,
                                     in_=cpf[:, _CPF_L2W:_CPF_L2W + 1024]), "dl")
                ch.dtick(s.dma_start(out=c_bc1,
                                     in_=cpf[:, _CPF_L2B:_CPF_L2B + 1024]), "dl")
                return None
            G("sync", l2ld)
            dll2 = ch.dval("dl")
            G("vector", lambda v: v.reciprocal(v_tmp2[:, 3:4], v_tmp2[:, 2:3]))
            G("vector", lambda v: v.tensor_scalar(v_tmp3, v_tmp3, v_tmp2[:, 0:1],
                                                  v_tmp2[:, 3:4], AL.subtract,
                                                  AL.mult), dwaits=[("ds", DS_HSP)])
            G("vector", lambda v: v.tensor_tensor(v_tmp3, v_tmp3, c_bc0,
                                                  AL.mult), dwaits=[("dl", dll2)])
            G("vector", lambda v: v.tensor_tensor(v_tmp3, v_tmp3, c_bc1, AL.add))
            G("vector", lambda v: v.tensor_copy(b_hi[:, 0:1024], v_tmp3))

            def hist(s, tc=tc):
                ch.dtick(s.dma_start(out=ag1_in[tc * 128:(tc + 1) * 128, :],
                                     in_=b_hi[:, 0:1024]), "ds")
                return None
            G("sync", hist)

            def htp8(t):
                for cc8 in range(8):
                    t.transpose(pst[:, cc8 * 128:(cc8 + 1) * 128],
                                b_hi[:, cc8 * 128:(cc8 + 1) * 128], c_id)
                return None
            G("tensor", htp8)
            G("vector", lambda v: v.tensor_copy(
                b_hiT, pst[:, 0:1024].rearrange("p (a k) -> p a k", a=8)))

            def lmm(t, tc=tc):
                for ds in range(8):
                    t.matmul(pss[0:8, 0:128], v_rwb[:, ds, :], b_hiT[:, ds, :],
                             start=(ds == 0), stop=(ds == 7))
                return None
            G("tensor", lmm)
            G("vector", lambda v, tc=tc: v.tensor_copy(
                b_logits[0:8, tc * 128:(tc + 1) * 128], pss[0:8, 0:128]))

    b_ltok = arB.at(16 * KB, [128, 16, 8], F32)
    for tc in range(16):
        G("tensor", lambda t, tc=tc: t.transpose(
            pss[:, 0:8], b_logits[0:8, tc * 128:(tc + 1) * 128], c_idf[0:8, 0:8]))
        G("vector", lambda v, tc=tc: v.tensor_copy(b_ltok[:, tc, :], pss[:, 0:8]))

    def l2st(s):
        ch.dtick(s.dma_start(out=ag2_in.rearrange("(j q) e -> q j e", q=128),
                             in_=b_ltok), "ds")
        return None
    G("sync", l2st)

    def ags(g):
        g.wait_ge(ch.dsem("ds")[0], ch.dval("ds"))
        cs = ch.dsem("cc")
        g.collective_compute(
            "AllGather", AL.bypass, replica_groups=[list(range(N_CORES))],
            ins=[ag1_in.ap().opt()],
            outs=[ag1_out.ap()[0:N_CORES * NTOK, :].opt()]).then_inc(cs[0], 1)
        cs[1] += 1
        g.collective_compute(
            "AllGather", AL.bypass, replica_groups=[list(range(N_CORES))],
            ins=[ag2_in.ap().opt()], outs=[ag2_out.ap().opt()]).then_inc(cs[0], 1)
        cs[1] += 1
        g.wait_ge(cs[0], cs[1])
        g.memset(v_tmp2[0:1, 0:512], 0.0)
        g.drain()
        ch.dtick(g.dma_start(out=ag1_out[N_CORES * NTOK:N_CORES * NTOK + 1, :],
                             in_=v_tmp2[0:1, 0:512].bitcast(BF16)), "dg")
        return None
    G("gpsimd", ags)
    DG_AGS = ch.dval("dg")

    d_w1 = arX.at(0, [128, 8, FFN], BF16)
    w2r_l = ([arB.at(8 * KB + fs * 2 * KB, [128, 1024], BF16) for fs in range(12)]
             + [arQ.at(16 * KB + (fs - 12) * 2 * KB, [128, 1024], BF16)
                for fs in range(12, 20)]
             + [arS.at((fs - 20) * 2 * KB, [128, 1024], BF16)
                for fs in range(20, 32)])
    w2e3 = w2e.rearrange("(c p) n -> p c n", p=128)

    def wldA(g):
        ch.dtick(g.dma_start(out=d_w1, in_=w1e.rearrange("(c p) n -> p c n",
                                                         p=128)), "dg")
        for fs in range(12, 32):
            ch.dtick(g.dma_start(out=w2r_l[fs], in_=w2e3[:, fs, :]), "dg")
        return None
    G("gpsimd", wldA, dwaits=[("ds", ch.dval("ds"))])
    DG_WE = ch.dval("dg")

    if stop_after == "stageB":
        emit_taps([("hafter", hsp.ap(), [NTOK, D], F32),
                   ("logits", b_ltok, [128, 16 * 8], F32)])
        return finish()

    r_pro = arB.at(0, [128, 128, 8], F32)
    r_Ep = [arB.at(4 * KB + i * KB, [128, 256], F32) for i in range(8)]
    r_gpair = arB.at(12 * KB, [128, 256], F32)
    r_epair = arB.at(13 * KB, [128, 256], F32)
    r_keep = arB.at(14 * KB, [128, 256], F32)
    r_slot = arB.at(15 * KB, [128, 256], F32)
    r_rank = arB.at(16 * KB, [128, 256], F32)
    r_incl = arB.at(17 * KB, [128, 256], F32)
    r_m = arB.at(18 * KB, [128, 128], F32)
    r_acc = arB.at(18 * KB + 512, [128, 128], F32)
    r_g1 = arB.at(19 * KB, [128, 128], F32)
    r_g2 = arB.at(19 * KB + 512, [128, 128], F32)
    c_tri2 = arQ.at(0, [128, 128], F32)
    c_rtok2 = arQ.at(512, [128, 128], F32)
    r_base = arM.at(2304, [128, 8], F32)
    r_tots = arM.at(2336, [128, 8], F32)
    r_own = arM.at(2368, [128, 32], F32)
    r_idx0 = arM.at(2560, [128, 16], U32)
    r_idx1 = arM.at(2624, [128, 16], U32)
    r_pidx = arM.at(2688, [128, 1], U32)
    r_iota = arM.at(2752, [128, 1], F32)

    def trld(s):
        ch.dtick(s.dma_start(out=c_tri2, in_=cpf[:, _CPF_TRI:_CPF_TRI + 128]),
                 "dl")
        ch.dtick(s.dma_start(out=c_rtok2, in_=cpf[:, _CPF_RTOK:_CPF_RTOK + 128]),
                 "dl")
        return None
    G("sync", trld, dwaits=[("dg", DG_AGS)])
    DL_TR = ch.dval("dl")

    def prold(s):
        for sg in range(2):
            for cg in range(8):
                pbase = 64 * sg + 8 * cg
                rbase = 2048 * cg + 1024 * sg
                ch.dtick(s.dma_start(
                    out=r_pro[pbase:pbase + 8, :, :],
                    in_=ag2_out[rbase:rbase + 1024, :]
                    .rearrange("(o i) e -> o i e", o=8)), "dl")
        return None
    G("sync", prold)
    DL_PRO = ch.dval("dl")

    G("vector", lambda v: v.tensor_reduce(r_m, r_pro, AX.X, AL.max),
      dwaits=[("dl", DL_PRO), ("dl", DL_TR)])

    def smax2(v):
        v.tensor_tensor(r_pro, r_pro,
                        r_m.unsqueeze(2).broadcast_to([128, 128, 8]), AL.subtract)
        return None
    G("vector", smax2)
    G("scalar", lambda sc: sc.activation(out=r_pro.rearrange("p a b -> p (a b)"),
                                         in_=r_pro.rearrange("p a b -> p (a b)"),
                                         func=AF.Exp))
    G("vector", lambda v: v.tensor_reduce(r_m, r_pro, AX.X, AL.add))
    G("vector", lambda v: v.reciprocal(r_m, r_m))

    def smax3(v):
        v.tensor_tensor(r_pro, r_pro, r_m.unsqueeze(2).broadcast_to([128, 128, 8]),
                        AL.mult)
        return None
    G("vector", smax3)

    def ep_k(e, k):
        return r_Ep[e].rearrange("p (a k) -> p a k", k=2)[:, :, k]

    def epair_k(k):
        return r_epair.rearrange("p (a k) -> p a k", k=2)[:, :, k]

    G("vector", lambda v: v.tensor_reduce(r_m, r_pro, AX.X, AL.max))
    G("gpsimd", lambda g: g.memset(r_acc, 0.0))
    G("gpsimd", lambda g: g.memset(r_epair, 0.0))
    for e in range(8):
        G("vector", lambda v, e=e: v.tensor_tensor(v_tmp3[:, 0:128],
                                                   r_pro[:, :, e], r_m, AL.is_ge))
        G("vector", lambda v: v.tensor_scalar(v_tmp2[:, 0:128], r_acc, -1.0, 1.0,
                                              AL.mult, AL.add))
        G("vector", lambda v: v.tensor_tensor(v_tmp3[:, 0:128], v_tmp3[:, 0:128],
                                              v_tmp2[:, 0:128], AL.mult))
        G("vector", lambda v, e=e: v.tensor_copy(ep_k(e, 0), v_tmp3[:, 0:128]))
        G("vector", lambda v: v.tensor_tensor(r_acc, r_acc, v_tmp3[:, 0:128],
                                              AL.add))
        if e > 0:
            G("vector", lambda v, e=e: v.tensor_scalar(v_tmp3[:, 0:128],
                                                       v_tmp3[:, 0:128], float(e),
                                                       None, AL.mult))
            G("vector", lambda v, e=e: v.tensor_tensor(epair_k(0), epair_k(0),
                                                       v_tmp3[:, 0:128], AL.add))
    G("vector", lambda v: v.tensor_copy(r_g1, r_m))
    for e in range(8):
        G("vector", lambda v, e=e: v.scalar_tensor_tensor(
            r_pro[:, :, e], ep_k(e, 0), -1e30, r_pro[:, :, e], AL.mult, AL.add))
    G("vector", lambda v: v.tensor_reduce(r_m, r_pro, AX.X, AL.max))
    G("vector", lambda v: v.tensor_copy(r_g2, r_m))
    G("gpsimd", lambda g: g.memset(r_acc, 0.0))
    for e in range(8):
        G("vector", lambda v, e=e: v.tensor_tensor(v_tmp3[:, 0:128],
                                                   r_pro[:, :, e], r_m, AL.is_ge))
        G("vector", lambda v: v.tensor_scalar(v_tmp2[:, 0:128], r_acc, -1.0, 1.0,
                                              AL.mult, AL.add))
        G("vector", lambda v: v.tensor_tensor(v_tmp3[:, 0:128], v_tmp3[:, 0:128],
                                              v_tmp2[:, 0:128], AL.mult))
        G("vector", lambda v, e=e: v.tensor_copy(ep_k(e, 1), v_tmp3[:, 0:128]))
        G("vector", lambda v: v.tensor_tensor(r_acc, r_acc, v_tmp3[:, 0:128],
                                              AL.add))
        if e > 0:
            G("vector", lambda v, e=e: v.tensor_scalar(v_tmp3[:, 0:128],
                                                       v_tmp3[:, 0:128], float(e),
                                                       None, AL.mult))
            G("vector", lambda v, e=e: v.tensor_tensor(epair_k(1), epair_k(1),
                                                       v_tmp3[:, 0:128], AL.add))
    G("vector", lambda v: v.tensor_tensor(v_tmp2[:, 0:128], r_g1, r_g2, AL.add))
    G("vector", lambda v: v.reciprocal(v_tmp2[:, 0:128], v_tmp2[:, 0:128]))
    G("vector", lambda v: v.tensor_tensor(r_g1, r_g1, v_tmp2[:, 0:128], AL.mult))
    G("vector", lambda v: v.tensor_tensor(r_g2, r_g2, v_tmp2[:, 0:128], AL.mult))

    def gpairk(k):
        return r_gpair.rearrange("p (a k) -> p a k", k=2)[:, :, k]

    G("vector", lambda v: v.tensor_copy(gpairk(0), r_g1))
    G("vector", lambda v: v.tensor_copy(gpairk(1), r_g2))

    for e in range(8):
        G("vector", lambda v, e=e: v.tensor_tensor_scan(
            r_incl, c_onesf[:, 0:256], r_Ep[e], 0.0, AL.mult, AL.add))
        G("vector", lambda v, e=e: v.tensor_copy(r_tots[:, e:e + 1],
                                                 r_incl[:, 255:256]))
    G("tensor", lambda t: t.matmul(pss[:, 0:8], c_tri2, r_tots, start=True,
                                   stop=True))
    G("vector", lambda v: v.tensor_copy(r_base, pss[:, 0:8]))

    G("gpsimd", lambda g: g.memset(r_rank, 0.0))
    for e in range(8):
        G("vector", lambda v, e=e: v.tensor_tensor_scan(
            r_incl, c_onesf[:, 0:256], r_Ep[e], 0.0, AL.mult, AL.add))
        G("vector", lambda v, e=e: v.tensor_tensor(r_incl, r_incl, r_Ep[e],
                                                   AL.subtract))
        G("vector", lambda v, e=e: v.tensor_scalar(r_incl, r_incl,
                                                   r_base[:, e:e + 1], None,
                                                   AL.add))
        G("vector", lambda v, e=e: v.tensor_tensor(r_incl, r_incl, r_Ep[e],
                                                   AL.mult))
        G("vector", lambda v, e=e: v.tensor_tensor(r_rank, r_rank, r_incl,
                                                   AL.add))

    G("vector", lambda v: v.tensor_scalar(r_keep, r_rank, float(CAP), None,
                                          AL.is_lt))
    G("vector", lambda v: v.tensor_scalar(r_slot, r_epair, float(CAP), None,
                                          AL.mult))
    G("vector", lambda v: v.tensor_tensor(r_slot, r_slot, r_rank, AL.add))
    G("vector", lambda v: v.tensor_tensor(r_slot, r_slot, r_keep, AL.mult))
    G("vector", lambda v: v.tensor_scalar(v_tmp2[:, 0:256], r_keep,
                                          -float(NEXP * CAP), float(NEXP * CAP),
                                          AL.mult, AL.add))
    G("vector", lambda v: v.tensor_tensor(r_slot, r_slot, v_tmp2[:, 0:256],
                                          AL.add))

    def slst(s):
        ch.dtick(s.dma_start(out=scr_ps.rearrange("(p f) -> p f", p=128),
                             in_=r_slot), "ds")
        return None
    G("sync", slst)
    DS_SL = ch.dval("ds")

    G("gpsimd", lambda g: g.iota(r_iota, [[1, 1]], base=0, channel_multiplier=1,
                                 allow_small_or_imprecise_dtypes=True))
    G("vector", lambda v: v.tensor_scalar(v_tmp2[:, 0:1], r_iota, 64.0, None,
                                          AL.is_ge))
    G("vector", lambda v: v.scalar_tensor_tensor(v_tmp2[:, 1:2], v_tmp2[:, 0:1],
                                                 448.0, r_iota, AL.mult, AL.add))
    G("vector", lambda v: v.tensor_scalar(v_tmp2[:, 1:2], v_tmp2[:, 1:2],
                                          c_cv[:, 1:2], None, AL.add))
    G("vector", lambda v: v.tensor_copy(r_pidx, v_tmp2[:, 1:2]))

    def owng(g):
        ch.dtick(g.indirect_dma_start(
            out=r_own, out_offset=None, in_=scr_ps.rearrange("(a b) -> a b", b=32),
            in_offset=IndirectOffsetOnAxis(r_pidx, 0)), "dg")
        return None
    G("gpsimd", owng, dwaits=[("ds", DS_SL)])
    DG_OWN = ch.dval("dg")

    def ownst(s):
        ch.dtick(s.dma_start(out=scr_own.rearrange("(p f) -> p f", p=128),
                             in_=r_own), "ds")
        return None
    G("sync", ownst, dwaits=[("dg", DG_OWN)])
    DS_OWN = ch.dval("ds")

    for k in range(2):
        def cidld(s, k=k):
            a3 = scr_own.rearrange("(m t k) -> t m k", t=128, k=2)
            ch.dtick(s.dma_start(out=v_tmp2[:, 0:16], in_=a3[:, :, k],
                                 allow_slow_non_contiguous=True), "dl")
            return None
        G("sync", cidld, dwaits=[("ds", DS_OWN)])
        dlc = ch.dval("dl")
        G("vector", (lambda k: lambda v: v.tensor_copy(
            (r_idx0 if k == 0 else r_idx1), v_tmp2[:, 0:16]))(k),
          dwaits=[("dl", dlc)])

    sc_tab = dram("sc_tab", [5248 * 2], F32)
    G("vector", lambda v: v.tensor_tensor(v_tmp2[:, 0:256], r_epair,
                                          c_cv[:, 0:1].broadcast_to([128, 256]),
                                          AL.is_equal))
    G("vector", lambda v: v.tensor_tensor(v_tmp2[:, 0:256], v_tmp2[:, 0:256],
                                          r_keep, AL.mult))
    mk3 = v_tmp2[:, 0:256].rearrange("p (a k) -> p a k", k=2)
    G("vector", lambda v: v.tensor_reduce(v_tmp2[:, 256:384], mk3, AX.X, AL.add))
    G("vector", lambda v: v.tensor_tensor(v_tmp3[:, 0:256], r_rank,
                                          v_tmp2[:, 0:256], AL.mult))
    G("vector", lambda v: v.tensor_reduce(
        v_tmp2[:, 384:512], v_tmp3[:, 0:256].rearrange("p (a k) -> p a k", k=2),
        AX.X, AL.add))
    G("vector", lambda v: v.tensor_tensor(v_tmp3[:, 0:256], r_gpair,
                                          v_tmp2[:, 0:256], AL.mult))
    G("vector", lambda v: v.tensor_reduce(
        v_tmp2[:, 512:640], v_tmp3[:, 0:256].rearrange("p (a k) -> p a k", k=2),
        AX.X, AL.add))
    G("vector", lambda v: v.tensor_scalar(v_tmp2[:, 640:641], r_iota, 1.0,
                                          5120.0, AL.mult, AL.add))
    G("vector", lambda v: v.tensor_scalar(v_tmp2[:, 768:896], v_tmp2[:, 256:384],
                                          -1.0, 1.0, AL.mult, AL.add))
    G("vector", lambda v: v.tensor_scalar(v_tmp2[:, 768:896], v_tmp2[:, 768:896],
                                          v_tmp2[:, 640:641], None, AL.mult))
    G("vector", lambda v: v.tensor_tensor(v_tmp2[:, 768:896], v_tmp2[:, 768:896],
                                          v_tmp2[:, 384:512], AL.add))
    r_scidx = arB.at(21 * KB, [128, 128], U32)
    G("vector", lambda v: v.tensor_copy(r_scidx, v_tmp2[:, 768:896]))
    r_pay = arB.at(22 * KB, [128, 256], F32)

    def payb2(v):
        p3 = r_pay.rearrange("p (a q) -> p a q", q=2)
        v.tensor_copy(p3[:, :, 0:1].squeeze(2), c_rtok2)
        v.tensor_copy(p3[:, :, 1:2].squeeze(2), v_tmp2[:, 512:640])
        return None
    G("vector", payb2)

    def tz(g):
        g.memset(v_tmp3[:, 0:82], 0.0)
        g.drain()
        ch.dtick(g.dma_start(out=sc_tab.rearrange("(p f) -> p f", p=128),
                             in_=v_tmp3[:, 0:82]), "dgs")
        return None
    G("gpsimd", tz)
    DGS_TZ = ch.dval("dgs")

    def scat(g):
        g.wait_ge(ch.dsem("dgs")[0], DGS_TZ)
        t2 = sc_tab.rearrange("(a b) -> a b", b=2)
        for c in range(128):
            ch.dtick(g.indirect_dma_start(
                out=t2, out_offset=IndirectOffsetOnAxis(r_scidx[:, c:c + 1], 0),
                in_=r_pay[:, 2 * c:2 * c + 2], in_offset=None), "dg")
        return None
    G("gpsimd", scat)
    DG_SCAT = ch.dval("dg")

    d_gidxf = arM.at(3072, [128, 40], F32)
    d_gidx = arM.at(3264, [128, 40], U32)
    r_gcol = arM.at(3456, [128, 40], F32)

    def rb(s):
        a2 = sc_tab.rearrange("(m p k) -> p m k", p=128, k=2)
        ch.dtick(s.dma_start(out=d_gidxf, in_=a2[:, 0:40, 0],
                             allow_slow_non_contiguous=True), "dl")
        ch.dtick(s.dma_start(out=r_gcol, in_=a2[:, 0:40, 1],
                             allow_slow_non_contiguous=True), "dl")
        return None
    G("sync", rb, dwaits=[("dg", DG_SCAT)])
    DL_RB = ch.dval("dl")
    G("vector", lambda v: v.tensor_copy(d_gidx, d_gidxf), dwaits=[("dl", DL_RB)])

    def wldB(g):
        for fs in range(12):
            ch.dtick(g.dma_start(out=w2r_l[fs], in_=w2e3[:, fs, :]), "dg")
        return None
    G("gpsimd", wldB)
    DG_WB = ch.dval("dg")

    if stop_after == "stageC":
        emit_taps([("slot", r_slot, [128, 256], F32),
                   ("rank", r_rank, [128, 256], F32),
                   ("ag2", ag2_out.ap(), [N_CORES * NTOK, NEXP], F32),
                   ("gidx", d_gidx, [128, 40], U32),
                   ("gcol", r_gcol, [128, 40], F32),
                   ("idx0", r_idx0, [128, 16], U32),
                   ("idx1", r_idx1, [128, 16], U32)])
        return finish()

    d_h1 = arA.at(0, [128, 32, TGRP], BF16)
    d_xe = arQ.at(2 * KB, [128, 8, TGRP], BF16)
    d_o2 = arQ.at(10 * KB + 256, [128, 1024], BF16)
    d_o2b = arQ.at(14 * KB, [128, 1024], BF16)
    d_xtok4 = arB.at(0, [128, 4, 1024], BF16)

    def gath(g, grp):
        for i4 in range(4):
            ch.dtick(g.indirect_dma_start(
                out=d_xtok4[:, i4, :], out_offset=None, in_=ag1_out.ap(),
                in_offset=IndirectOffsetOnAxis(
                    d_gidx[:, grp * 4 + i4:grp * 4 + i4 + 1], 0)), "dg")
        return None

    def xpose(grp, dgx):
        for i4 in range(4):
            def tp(t, i4=i4):
                for cc8 in range(8):
                    t.transpose(pst[:, cc8 * 128:(cc8 + 1) * 128],
                                d_xtok4[:, i4, cc8 * 128:(cc8 + 1) * 128], c_id)
                return None
            G("tensor", tp, dwaits=[("dg", dgx)])
            G("vector", lambda v, i4=i4: v.tensor_copy(
                d_xe[:, :, i4 * 128:(i4 + 1) * 128],
                pst[:, 0:1024].rearrange("p (a k) -> p a k", a=8)))

    G("gpsimd", lambda g: gath(g, 0))
    DGX = [ch.dval("dg")] + [None] * NGRP
    xpose(0, DGX[0])

    for grp in range(NGRP):
        if grp == 0:
            txe = tap("xe0", [128, 8 * TGRP], BF16)
            if txe is not None:
                G("sync", lambda s, txe=txe: ch.dtick(s.dma_start(
                    out=txe, in_=d_xe.rearrange("p a b -> p (a b)")), "ds")
                    and None)

        for k in range(17):
            specs = []
            if k < 16:
                def w1mm(t, k=k):
                    psA = ps0 if k % 2 == 0 else ps1
                    for sub in range(2):
                        fs = 2 * k + sub
                        for ds in range(8):
                            t.matmul(psA[:, sub * TGRP:(sub + 1) * TGRP],
                                     d_w1[:, ds, fs * 128:(fs + 1) * 128],
                                     d_xe[:, ds, :], start=(ds == 0),
                                     stop=(ds == 7))
                    return None
                specs.append(("tensor", w1mm,
                              ["dg"] if (grp == 0 and k == 0) else []))
            if k >= 1:
                def gel(sc, k=k - 1):
                    psB = ps0 if k % 2 == 0 else ps1
                    sc.activation(
                        out=d_h1[:, 2 * k:2 * k + 2, :].rearrange(
                            "p a b -> p (a b)"),
                        in_=psB[:, 0:1024], func=AF.Gelu)
                    return None
                specs.append(("scalar", gel, []))
            P(specs)
        if grp == 0:
            th1 = tap("h10", [128, 2 * TGRP], BF16)
            if th1 is not None:
                G("sync", lambda s, th1=th1: ch.dtick(s.dma_start(
                    out=th1, in_=d_h1[:, 0:2, :].rearrange("p a b -> p (a b)")),
                    "ds") and None)

        ps_rot = [ps0, ps1, psz, ps0]
        d_o2p = [d_o2, d_o2b]

        def w2mm_f(sub):
            def w2mm(t, sub=sub):
                psx = ps_rot[sub]
                for fs in range(32):
                    mmN(t, psx, d_h1[:, fs, sub * 128:(sub + 1) * 128],
                        w2r_l[fs], fs == 0, fs == 31)
                return None
            return w2mm

        def o2ev_f(sub):
            def o2ev(v, grp=grp, sub=sub):
                gc = r_gcol[:, grp * 4 + sub:grp * 4 + sub + 1]
                v.tensor_scalar(d_o2p[sub % 2][:, 0:1024],
                                ps_rot[sub][:, 0:1024], gc, None, AL.mult)
                return None
            return o2ev

        def o2st_f(sub):
            def o2st(s, grp=grp, sub=sub):
                r0 = grp * TGRP + sub * 128
                ch.dtick(s.dma_start(out=agE_in[r0:r0 + 128, :],
                                     in_=d_o2p[sub % 2][:, 0:1024]), "ds")
                return None
            return o2st

        for sub in range(4):
            specs = [("tensor", w2mm_f(sub),
                      ["dg"] if (grp == 0 and sub == 0) else [])]
            if sub == 0 and grp + 1 < NGRP:
                specs.append(("gpsimd",
                              (lambda grp: lambda g: gath(g, grp + 1))(grp), []))
            if sub >= 1:
                specs.append(("vector", o2ev_f(sub - 1), ["ds"]))
            if sub >= 2:
                specs.append(("sync", o2st_f(sub - 2), []))
            P(specs)
            if sub == 0 and grp + 1 < NGRP:
                DGX[grp + 1] = ch.dval("dg")
        P([("vector", o2ev_f(3), ["ds"]), ("sync", o2st_f(2), [])])
        G("sync", o2st_f(3))

        if grp + 1 < NGRP:
            xpose(grp + 1, DGX[grp + 1])

    def age(g):
        g.wait_ge(ch.dsem("ds")[0], ch.dval("ds"))
        cs = ch.dsem("cc")
        g.collective_compute(
            "AllGather", AL.bypass, replica_groups=[list(range(N_CORES))],
            ins=[agE_in.ap().opt()],
            outs=[agE_out.ap()[0:NEXP * CAP, :].opt()]).then_inc(cs[0], 1)
        cs[1] += 1
        g.wait_ge(cs[0], cs[1])
        g.memset(v_tmp2[0:1, 0:512], 0.0)
        g.drain()
        ch.dtick(g.dma_start(out=agE_out[NEXP * CAP:NEXP * CAP + 1, :],
                             in_=v_tmp2[0:1, 0:512].bitcast(BF16)), "dg")
        return None
    G("gpsimd", age)
    DG_AGE = ch.dval("dg")

    if stop_after == "stageD":
        emit_taps([("agein", agE_in.ap(), [CAP, D], BF16),
                   ("gidx", d_gidx, [128, 40], U32),
                   ("gcol", r_gcol, [128, 40], F32)])
        return finish()

    e_g0p = [arQ.at(12 * KB, [128, 1024], BF16), arQ.at(16 * KB, [128, 1024],
                                                        BF16)]
    e_g1p = [arQ.at(14 * KB, [128, 1024], BF16), arQ.at(18 * KB, [128, 1024],
                                                        BF16)]
    e_hsp = [arB.at(8 * KB, [128, 1024], F32), arB.at(12 * KB, [128, 1024], F32)]

    def cgf(tc):
        def cg0(g, tc=tc):
            pb = tc % 2
            ch.dtick(g.indirect_dma_start(
                out=e_g0p[pb], out_offset=None, in_=agE_out.ap(),
                in_offset=IndirectOffsetOnAxis(r_idx0[:, tc:tc + 1], 0)), "dg")
            ch.dtick(g.indirect_dma_start(
                out=e_g1p[pb], out_offset=None, in_=agE_out.ap(),
                in_offset=IndirectOffsetOnAxis(r_idx1[:, tc:tc + 1], 0)), "dg")
            return None
        return cg0

    def hldf(tc):
        def hld(s, tc=tc):
            ch.dtick(s.dma_start(out=e_hsp[tc % 2],
                                 in_=hsp[tc * 128:(tc + 1) * 128, :]), "dl")
            return None
        return hld

    G("gpsimd", cgf(0), dwaits=[("dg", DG_AGE)])
    G("sync", hldf(0))
    for tc in range(16):
        pb = tc % 2
        specs = [("vector", (lambda pb: lambda v: v.tensor_tensor(
            v_tmp3, e_g0p[pb], e_g1p[pb], AL.add))(pb), ["dg", "ds"])]
        if tc + 1 < 16:
            specs.append(("gpsimd", cgf(tc + 1), []))
            specs.append(("sync", hldf(tc + 1), []))
        P(specs)
        G("vector", (lambda pb: lambda v: v.tensor_tensor(
            v_tmp3, v_tmp3, e_hsp[pb], AL.add))(pb), dwaits=[("dl", 0)])

        G("vector", lambda v: v.tensor_reduce(v_tmp2[:, 0:1], v_tmp3, AX.X,
                                              AL.add))
        G("scalar", lambda sc: sc.activation(out=v_xrch, in_=v_tmp3,
                                             func=AF.Square))
        G("vector", lambda v: v.tensor_reduce(v_tmp2[:, 1:2], v_xrch, AX.X,
                                              AL.add))
        G("vector", lambda v: v.tensor_scalar_mul(v_tmp2[:, 0:2], v_tmp2[:, 0:2],
                                                  1.0 / 1024.0))
        G("vector", lambda v: v.tensor_tensor(v_tmp2[:, 2:3], v_tmp2[:, 0:1],
                                              v_tmp2[:, 0:1], AL.mult))
        G("vector", lambda v: v.tensor_tensor(v_tmp2[:, 1:2], v_tmp2[:, 1:2],
                                              v_tmp2[:, 2:3], AL.subtract))
        G("scalar", lambda sc: sc.activation(out=v_tmp2[:, 2:3],
                                             in_=v_tmp2[:, 1:2], func=AF.Sqrt,
                                             bias=c_eps, scale=1.0))
        G("vector", lambda v: v.reciprocal(v_tmp2[:, 3:4], v_tmp2[:, 2:3]))
        G("vector", lambda v: v.tensor_scalar(v_tmp3, v_tmp3,
                                              v_tmp2[:, 0:1], v_tmp2[:, 3:4],
                                              AL.subtract, AL.mult))

        def ost(s, tc=tc):
            ch.dtick(s.dma_start(out=out[tc * 128:(tc + 1) * 128, :], in_=v_tmp3),
                     "ds")
            return None
        G("sync", ost)

    def fin(s):
        s.wait_ge(ch.dsem("ds")[0], ch.dval("ds"))
        return None
    G("sync", fin)
    return tap_out


def _constpacks(core):
    cpf = np.zeros((128, CPF), np.float32)
    ii, jj = np.meshgrid(np.arange(128), np.arange(128), indexing="ij")
    cpf[:, _CPF_TRI:_CPF_TRI + 128] = (jj > ii).astype(np.float32)
    cpf[0:64, _CPF_EYER:_CPF_EYER + 64] = np.eye(64, dtype=np.float32) * 1e-6
    pvec = np.arange(128)
    s = pvec // 64
    cc = (pvec % 64) // 8
    oo = pvec % 8
    base = 2048 * cc + 1024 * s + 128 * oo
    cpf[:, _CPF_RTOK:_CPF_RTOK + 128] = (base[:, None] + np.arange(128)[None, :])
    cpf[:, _CPF_CV + 0] = core
    cpf[:, _CPF_CV + 1] = 64 * core
    cpf[:, _CPF_CV + 2] = 1024 * core
    cpf[:, _CPF_CV + 3] = 2048 * core
    cpf[:, _CPF_ONES:_CPF_ONES + 256] = 1.0
    cpf[:, _CPF_IDF:_CPF_IDF + 128] = np.eye(128, dtype=np.float32)
    cpb = np.zeros((128, CPB), NPBF)
    cpb[:, _CPB_ID:_CPB_ID + 128] = np.eye(128, dtype=np.float32).astype(NPBF)
    cpb[:, _CPB_E2A:_CPB_E2A + 128] = (2 * np.eye(128, dtype=np.float32)
                                       ).astype(NPBF)
    blk2 = np.zeros((128, 2), np.float32)
    blk2[0:64, 0] = 1.0
    blk2[64:128, 1] = 1.0
    cpb[:, _CPB_BLK2:_CPB_BLK2 + 2] = blk2.astype(NPBF)
    cpb[:, _CPB_ONES:_CPB_ONES + 256] = np.float32(1.0).astype(NPBF)
    return cpf, cpb


def _fill_bias(cpf, inputs):
    def bc(col, vec):
        cpf[:, col:col + 1024] = np.asarray(vec, np.float32)[None, :]
    bc(_CPF_BV, inputs["L0_attn_bv"])
    bc(_CPF_BO, inputs["L0_attn_bo"])
    bc(_CPF_L1W, inputs["L0_ffn_ln1_weight"])
    bc(_CPF_L1B, inputs["L0_ffn_ln1_bias"])
    bc(_CPF_L2W, inputs["L0_ffn_ln2_weight"])
    bc(_CPF_L2B, inputs["L0_ffn_ln2_bias"])
    cpf[:, _CPF_BQC:_CPF_BQC + 8] = np.asarray(
        inputs["L0_attn_bq"], np.float32).reshape(8, 128).T
    cpf[:, _CPF_BKC:_CPF_BKC + 8] = np.asarray(
        inputs["L0_attn_bk"], np.float32).reshape(8, 128).T


def make_in_maps(inputs):
    q = np.asarray(inputs["L0_attn_query"], np.float32)
    wq = np.ascontiguousarray(np.asarray(inputs["L0_attn_Wq"], np.float32).T)
    wk = np.ascontiguousarray(np.asarray(inputs["L0_attn_Wk"], np.float32).T)
    wv = np.ascontiguousarray(np.asarray(inputs["L0_attn_Wv"], np.float32).T)
    wo = np.ascontiguousarray(np.asarray(inputs["L0_attn_Wo"], np.float32).T)
    qkvw = np.ascontiguousarray(
        np.asarray(inputs["L0_ffn_qkv_weight"], np.float32).T)
    proj = np.ascontiguousarray(
        np.asarray(inputs["L0_ffn_proj_weight"], np.float32).T)
    rw = np.ascontiguousarray(np.asarray(inputs["L0_ffn_router_weight"],
                                         np.float32))
    mw1 = np.asarray(inputs["L0_ffn_moe_w1"], np.float32)
    mw2 = np.asarray(inputs["L0_ffn_moe_w2"], np.float32)
    lmi = np.asarray(inputs["L0_attn_landmark_idx"], np.int32)
    tpr = np.asarray(inputs["L0_attn_temperature"], np.float32).reshape(1)
    in_maps = []
    for c in range(N_CORES):
        xrc = np.ascontiguousarray(
            q[:, 1024 * c:1024 * (c + 1), :].reshape(NTOK, D))
        xtc = np.ascontiguousarray(xrc.T)
        cpf, cpb = _constpacks(c)
        _fill_bias(cpf, inputs)
        in_maps.append({
            "xr": xrc, "xt": xtc, "wq_t": wq, "wk_t": wk, "wv_t": wv,
            "wo_t": wo, "qkvw_t": qkvw, "proj_t": proj, "rww": rw,
            "w1e": np.ascontiguousarray(mw1[c]),
            "w2e": np.ascontiguousarray(mw2[c]),
            "lmidxT": lmi, "tempr": tpr, "cpf": cpf, "cpb": cpb,
        })
    return in_maps


def kernel(**inputs):
    nc = bass.Bass("TRN2", target_bir_lowering=False)
    build(nc)
    in_maps = make_in_maps(inputs)
    res = run_bass_kernel_spmd(nc, in_maps, core_ids=list(range(N_CORES)))
    outs = [res.results[c]["out"].reshape(2, 1024, D) for c in range(N_CORES)]
    return np.concatenate(outs, axis=1)
